# version 01 (final)
# baseline (speedup 1.0000x reference)
import numpy as np
import ml_dtypes

import concourse.bass as bass
import concourse.mybir as mybir
from concourse.bass_utils import run_bass_kernel_spmd

F32 = mybir.dt.float32
BF16 = mybir.dt.bfloat16
I32 = mybir.dt.int32
AF = mybir.ActivationFunctionType
ALU = mybir.AluOpType
AXX = mybir.AxisListType.X

NCORES = 8
TPC = 1024
D = 4096
SEQ = 8192
HD = 128
SM = float(HD) ** -0.5
EPS = 1e-5
ENGS = ("sync", "gpsimd", "tensor", "vector", "scalar")

_CONST_BIAS = [False]
QT_A = 512
NKT_A = 12

MASKED_A = {}
for _qt in range(2):
    s = set()
    for _ki in range(NKT_A):
        _d = _ki * 128 - 1024
        if _d >= 0 or _d <= -640:
            s.add(_ki)
        if _qt * QT_A + _d < 0:
            s.add(_ki)
    MASKED_A[_qt] = sorted(s)
MASK_OFF = {}
_off = 0
for _qt in range(2):
    MASK_OFF[_qt] = _off
    _off += len(MASKED_A[_qt])
N_MASK = _off


class Prog:

    def __init__(self):
        self.items = {e: [] for e in ENGS}
        self.cnt = {}

    def val(self, sem):
        return self.cnt.get(sem.name, 0)

    def wait(self, eng, sem, v):
        if v > 0:
            self.items[eng].append(("w", sem, v))

    def op(self, eng, fn, inc=None, n=1):
        rec = None
        if inc is not None:
            self.cnt[inc.name] = self.val(inc) + n
            rec = (inc, n)
        self.items[eng].append(("o", fn, rec))
        return self.cnt[inc.name] if inc is not None else None

    def dma(self, eng, out, in_, sem):
        return self.op(eng, (lambda E, o=out, i=in_: E.dma_start(out=o, in_=i)),
                       inc=sem, n=16)

    def seminc(self, eng, sem, n=1):
        self.cnt[sem.name] = self.val(sem) + n
        self.items[eng].append(("i", sem, n))
        return self.cnt[sem.name]

    def emit(self, eng, E):
        for it in self.items[eng]:
            k = it[0]
            if k == "w":
                E.wait_ge(it[1], it[2])
            elif k == "i":
                E.sem_inc(it[1], it[2])
            else:
                inst = it[1](E)
                if it[2] is not None:
                    inst.then_inc(it[2][0], it[2][1])


class MK:
    def __init__(self, nc, ctx, debug_stop=0):
        self.nc = nc
        self.P = Prog()
        self.debug_stop = debug_stop
        E = ctx.enter_context
        dram = nc.dram_tensor

        self.x_t = dram("x_t", [D, TPC], F32, kind="ExternalInput")
        self.xp_t = dram("xp_t", [D, TPC], F32, kind="ExternalInput")
        self.wq_t = dram("wq_t", [D, 4096], F32, kind="ExternalInput")
        self.wk_t = dram("wk_t", [D, 1024], F32, kind="ExternalInput")
        self.wv_t = dram("wv_t", [D, 1024], F32, kind="ExternalInput")
        self.wo_t = dram("wo_t", [4096, D], F32, kind="ExternalInput")
        self.bwq_t = dram("bwq_t", [D, 4096], F32, kind="ExternalInput")
        self.bwk_t = dram("bwk_t", [D, 4096], F32, kind="ExternalInput")
        self.bwv_t = dram("bwv_t", [D, 4096], F32, kind="ExternalInput")
        self.bwo_t = dram("bwo_t", [4096, D], F32, kind="ExternalInput")
        self.qw_t = dram("qw_t", [D, 256], I32, kind="ExternalInput")
        self.zr_t = dram("zr_t", [D, 512], F32, kind="ExternalInput")
        self.sr_t = dram("sr_t", [D, 512], F32, kind="ExternalInput")
        self.ropeC_t = dram("ropeC_t", [128, TPC], F32, kind="ExternalInput")
        self.ropeS_t = dram("ropeS_t", [128, TPC], F32, kind="ExternalInput")
        self.ropeCp_t = dram("ropeCp_t", [128, TPC], F32, kind="ExternalInput")
        self.ropeSp_t = dram("ropeSp_t", [128, TPC], F32, kind="ExternalInput")
        self.mask_t = dram("mask_t", [128, N_MASK * QT_A], BF16, kind="ExternalInput")
        self.ident_t = dram("ident_t", [128, 128], BF16, kind="ExternalInput")
        self.y_t = dram("y_t", [D, TPC], F32, kind="ExternalOutput")
        self.res0 = dram("res0", [D, TPC], F32)
        self.res1 = dram("res1", [D, TPC], F32)
        self.qbuf = dram("qbuf", [4096, TPC], BF16)
        self.obuf = dram("obuf", [4096, TPC], BF16)
        self.kabuf = dram("kabuf", [1024, 2048], BF16)
        self.vabuf = dram("vabuf", [2048, 1024], BF16)
        self.agKb_i = dram("agKb_i", [4096, TPC], BF16)
        self.agKb_o = dram("agKb_o", [32768, TPC], BF16)
        self.agVb_i = dram("agVb_i", [TPC, 4096], BF16)
        self.agVb_o = dram("agVb_o", [8192, 4096], BF16)
        self.agW_i = dram("agW_i", [512, 4096], BF16)
        self.agW_o = dram("agW_o", [4096, 4096], BF16)

        self.arA = E(nc.sbuf_tensor("arA", [128, 32768], BF16))
        self.arB = E(nc.sbuf_tensor("arB", [128, 16384], BF16))
        self.wblk = E(nc.sbuf_tensor("wblk", [128, 8192], BF16))
        self.rch = E(nc.sbuf_tensor("rch", [128, 4096], F32))
        self.xt2 = E(nc.sbuf_tensor("xt2", [128, 4096], BF16))
        self.ropeC = E(nc.sbuf_tensor("ropeC", [128, TPC], F32))
        self.ropeS = E(nc.sbuf_tensor("ropeS", [128, TPC], F32))
        self.maskT = E(nc.sbuf_tensor("maskT", [128, N_MASK * QT_A], BF16))
        self.ptile = E(nc.sbuf_tensor("ptile", [128, 2048], BF16))
        self.qh = E(nc.sbuf_tensor("qh", [128, 2048], BF16))
        self.stg = E(nc.sbuf_tensor("stg", [128, 2048], BF16))
        self.linvb = E(nc.sbuf_tensor("linvb", [128, 512], F32))
        self.tmp2 = E(nc.sbuf_tensor("tmp2", [128, 2048], F32))
        self.ident = E(nc.sbuf_tensor("ident", [128, 128], BF16))
        self.ones_bf = E(nc.sbuf_tensor("ones_bf", [128, 1], BF16))
        self.ones_f = E(nc.sbuf_tensor("ones_f", [1, 128], F32))
        self.musb = E(nc.sbuf_tensor("musb", [1, 1024], F32))
        self.mssb = E(nc.sbuf_tensor("mssb", [1, 1024], F32))
        self.tmpv = E(nc.sbuf_tensor("tmpv", [1, 1024], F32))
        self.rstdsb = E(nc.sbuf_tensor("rstdsb", [1, 1024], F32))
        self.sc = E(nc.sbuf_tensor("sc", [1, 64], F32))
        self.biasb = E(nc.sbuf_tensor("biasb", [128, 2], F32))
        self.qsq = E(nc.sbuf_tensor("qsq", [128, 1024], BF16))
        self.ksq = E(nc.sbuf_tensor("ksq", [128, 512], BF16))

        self.ps_a = E(nc.psum_tensor("ps_a", [128, 2048], F32))
        self.ps_b = E(nc.psum_tensor("ps_b", [128, 2048], F32))

        self.sDS = E(nc.semaphore("sDS"))
        self.sDG = E(nc.semaphore("sDG"))
        self.sPE = E(nc.semaphore("sPE"))
        self.sVE = E(nc.semaphore("sVE"))
        self.sAE = E(nc.semaphore("sAE"))
        self.sPO = E(nc.semaphore("sPO"))
        self.sCW = E(nc.semaphore("sCW"))
        self.sCB = E(nc.semaphore("sCB"))
        self.sBAR = E(nc.semaphore("sBAR"))
        self.nbar = 0
        self._ln_stats_done = 0

    def phase_guard(self):
        P = self.P
        P.wait("tensor", self.sVE, P.val(self.sVE))
        P.wait("tensor", self.sAE, P.val(self.sAE))

    def barrier(self):
        P = self.P
        P.wait("sync", self.sDS, P.val(self.sDS))
        P.wait("gpsimd", self.sDG, P.val(self.sDG))
        for e in ENGS:
            P.seminc(e, self.sBAR, 1)
        self.nbar += 1
        tgt = 5 * self.nbar
        for e in ENGS:
            P.wait(e, self.sBAR, tgt)

    def init_consts(self):
        P = self.P
        P.dma("sync", self.ropeC[:, :], self.ropeCp_t[:, :], self.sDS)
        P.dma("sync", self.ropeS[:, :], self.ropeSp_t[:, :], self.sDS)
        P.dma("sync", self.maskT[:, :], self.mask_t[:, :], self.sDS)
        self._ident_ld = P.dma("sync", self.ident[:, :], self.ident_t[:, :], self.sDS)
        P.op("vector", lambda E: E.memset(self.ones_bf[:, :], 1.0))
        self._ones_done = P.op("vector", lambda E: E.memset(self.ones_f[:, :], 1.0), inc=self.sVE)

    def dequant(self):
        P = self.P
        arB = self.arB
        zr = [arB[:, 0:1024].bitcast(F32), arB[:, 1024:2048].bitcast(F32)]
        sr = [arB[:, 2048:3072].bitcast(F32), arB[:, 3072:4096].bitcast(F32)]
        qwi = [arB[:, 4096:4608].bitcast(I32), arB[:, 4608:5120].bitcast(I32)]
        nibi = arB[:, 5120:6144].bitcast(I32)
        nibf = arB[:, 6144:7168].bitcast(F32)
        wdeq = [arB[:, 7168:7680], arB[:, 7680:8192]]
        tpv = [self.ps_b[:, 0:64].bitcast(BF16), self.ps_b[:, 512:576].bitcast(BF16)]

        ld_mark = [0] * 32
        ve_mark = [0] * 32
        pe_mark = [0] * 32
        ae_mark = [0] * 32
        st_mark = [0] * 32
        for oc in range(32):
            b = oc % 2
            if oc >= 2:
                P.wait("sync", self.sVE, ve_mark[oc - 2])
            P.dma("sync", qwi[b][:, :], self.qw_t[oc * 128:(oc + 1) * 128, :], self.sDS)
            P.dma("sync", zr[b][:, :], self.zr_t[oc * 128:(oc + 1) * 128, :], self.sDS)
            ld_mark[oc] = P.dma("sync", sr[b][:, :], self.sr_t[oc * 128:(oc + 1) * 128, :], self.sDS)
            P.wait("vector", self.sDS, ld_mark[oc])
            if oc >= 2:
                P.wait("vector", self.sPE, pe_mark[oc - 2])
            P.op("vector", lambda E, b=b: E.tensor_single_scalar(nibi[:, 0:512:2], qwi[b][:, :], 15, ALU.bitwise_and))
            d1 = P.op("vector", lambda E, b=b: E.tensor_scalar(nibi[:, 1:512:2], qwi[b][:, :], 4, 15, ALU.logical_shift_right, ALU.bitwise_and), inc=self.sVE)
            P.wait("vector", self.sVE, d1)
            d2 = P.op("vector", lambda E: E.tensor_copy(nibf[:, :], nibi[:, :]), inc=self.sVE)
            P.wait("vector", self.sVE, d2)
            d3 = P.op("vector", lambda E, b=b: E.tensor_sub(nibf[:, :], nibf[:, :], zr[b][:, :]), inc=self.sVE)
            P.wait("vector", self.sVE, d3)
            ve_mark[oc] = P.op("vector", lambda E, b=b: E.tensor_mul(wdeq[b][:, :], nibf[:, :], sr[b][:, :]), inc=self.sVE)
            P.wait("tensor", self.sVE, ve_mark[oc])
            if oc == 0:
                P.wait("tensor", self.sDS, self._ident_ld)
            if oc >= 1:
                P.wait("tensor", self.sAE, ae_mark[oc - 1])
            stgh = self.stg[:, b * 1024:(b + 1) * 1024]
            am = 0
            am_isub = {}
            for isub in range(4):
                if isub >= 2:
                    P.wait("tensor", self.sAE, am_isub[isub - 2])
                pm = P.op("tensor", lambda E, b=b, isub=isub: E.transpose(
                    tpv[isub % 2][:, :], wdeq[b][:, isub * 128:(isub + 1) * 128], self.ident[:, :]),
                    inc=self.sPE)
                if isub == 3:
                    pe_mark[oc] = pm
                P.wait("scalar", self.sPE, pm)
                if oc >= 2 and isub == 0:
                    P.wait("scalar", self.sDS, st_mark[oc - 2])
                am = P.op("scalar", lambda E, isub=isub, stgh=stgh: E.copy(
                    stgh[:, isub * 128:(isub + 1) * 128], tpv[isub % 2][:, :]), inc=self.sAE)
                am_isub[isub] = am
            ae_mark[oc] = am
            P.wait("sync", self.sAE, ae_mark[oc])
            sm = 0
            for isub in range(4):
                sm = P.dma("sync", self.agW_i[isub * 128:(isub + 1) * 128, oc * 128:(oc + 1) * 128],
                           stgh[:, isub * 128:(isub + 1) * 128], self.sDS)
            st_mark[oc] = sm
        self._agw_wait = P.val(self.sDS)
        self._deq_ae_final = ae_mark[31]

    def fire_agw(self):
        P = self.P
        P.wait("gpsimd", self.sDS, self._agw_wait)
        P.op("gpsimd", lambda E: E.collective_compute(
            "AllGather", ALU.bypass, replica_groups=[list(range(NCORES))],
            ins=[self.agW_i.ap().opt()], outs=[self.agW_o.ap().opt()]), inc=self.sCW, n=1)

    def ln_stats_standalone(self, src, guard_deq=False):
        P = self.P
        self.phase_guard()
        xt = [self.xt2[:, 0:1024], self.xt2[:, 1024:2048]]
        xq = [self.xt2[:, 2048:3072], self.xt2[:, 3072:4096]]
        po_mark = [0] * 32
        pe_mark = [0] * 32
        for kc in range(32):
            b = kc % 2
            if kc >= 2:
                P.wait("sync", self.sPO, po_mark[kc - 2])
            ld = P.dma("sync", self.rch[:, b * 1024:(b + 1) * 1024],
                       src[kc * 128:(kc + 1) * 128, :], self.sDS)
            P.wait("gpsimd", self.sDS, ld)
            if kc >= 2:
                P.wait("gpsimd", self.sPE, pe_mark[kc - 2])
            rv = self.rch[:, b * 1024:(b + 1) * 1024]
            P.op("gpsimd", lambda E, rv=rv, b=b: E.tensor_copy(xt[b][:, :], rv))
            po_mark[kc] = P.op("gpsimd", lambda E, rv=rv, b=b: E.tensor_mul(xq[b][:, :], rv, rv), inc=self.sPO)
            P.wait("tensor", self.sPO, po_mark[kc])
            if kc == 0 and guard_deq:
                P.wait("tensor", self.sAE, self._deq_ae_final)
            st = kc == 0
            sp = kc == 31
            P.op("tensor", lambda E, b=b, st=st, sp=sp: E.matmul(
                self.ps_b[0:1, 0:512], self.ones_bf[:, :], xt[b][:, 0:512], start=st, stop=sp))
            P.op("tensor", lambda E, b=b, st=st, sp=sp: E.matmul(
                self.ps_b[0:1, 512:1024], self.ones_bf[:, :], xt[b][:, 512:1024], start=st, stop=sp))
            P.op("tensor", lambda E, b=b, st=st, sp=sp: E.matmul(
                self.ps_b[0:1, 1024:1536], self.ones_bf[:, :], xq[b][:, 0:512], start=st, stop=sp))
            pe_mark[kc] = P.op("tensor", lambda E, b=b, st=st, sp=sp: E.matmul(
                self.ps_b[0:1, 1536:2048], self.ones_bf[:, :], xq[b][:, 512:1024], start=st, stop=sp),
                inc=self.sPE)
        self._ln_stats_done = pe_mark[31]

    def ln_finalize_apply(self, src):
        P = self.P
        P.wait("vector", self.sPE, self._ln_stats_done)
        m1 = P.op("vector", lambda E: E.tensor_single_scalar(self.musb[:, :], self.ps_b[0:1, 0:1024], 1.0 / D, ALU.mult), inc=self.sVE)
        P.op("vector", lambda E: E.tensor_single_scalar(self.mssb[:, :], self.ps_b[0:1, 1024:2048], 1.0 / D, ALU.mult))
        P.wait("vector", self.sVE, m1)
        m2 = P.op("vector", lambda E: E.tensor_mul(self.tmpv[:, :], self.musb[:, :], self.musb[:, :]), inc=self.sVE)
        P.wait("vector", self.sVE, m2)
        m3 = P.op("vector", lambda E: E.tensor_sub(self.mssb[:, :], self.mssb[:, :], self.tmpv[:, :]), inc=self.sVE)
        P.wait("vector", self.sVE, m3)
        vm = P.op("vector", lambda E: E.tensor_single_scalar(self.mssb[:, :], self.mssb[:, :], EPS, ALU.add), inc=self.sVE)
        P.wait("scalar", self.sVE, vm)
        a1 = P.op("scalar", lambda E: E.activation(self.tmpv[:, :], self.mssb[:, :], AF.Ln), inc=self.sAE)
        P.wait("scalar", self.sAE, a1)
        am = P.op("scalar", lambda E: E.activation(self.rstdsb[:, :], self.tmpv[:, :], AF.Exp, scale=-0.5), inc=self.sAE)
        P.wait("tensor", self.sAE, am)
        P.op("tensor", lambda E: E.matmul(self.ps_b[:, 0:512], self.ones_f[:, :], self.musb[0:1, 0:512], start=True, stop=True))
        P.op("tensor", lambda E: E.matmul(self.ps_b[:, 512:1024], self.ones_f[:, :], self.rstdsb[0:1, 0:512], start=True, stop=True))
        P.op("tensor", lambda E: E.matmul(self.ps_b[:, 1024:1536], self.ones_f[:, :], self.musb[0:1, 512:1024], start=True, stop=True))
        bm = P.op("tensor", lambda E: E.matmul(self.ps_b[:, 1536:2048], self.ones_f[:, :], self.rstdsb[0:1, 512:1024], start=True, stop=True), inc=self.sPE)
        ve_mark = [0] * 32
        for kc in range(32):
            b = kc % 2
            if kc >= 2:
                P.wait("sync", self.sVE, ve_mark[kc - 2])
            ld = P.dma("sync", self.rch[:, b * 1024:(b + 1) * 1024],
                       src[kc * 128:(kc + 1) * 128, :], self.sDS)
            P.wait("vector", self.sDS, ld)
            if kc == 0:
                P.wait("vector", self.sPE, bm)
            rv = self.rch[:, b * 1024:(b + 1) * 1024]
            m = 0
            for tt in range(2):
                mu_v = self.ps_b[:, 0:512] if tt == 0 else self.ps_b[:, 1024:1536]
                rs_v = self.ps_b[:, 512:1024] if tt == 0 else self.ps_b[:, 1536:2048]
                s1 = P.op("vector", lambda E, rv=rv, tt=tt, mu_v=mu_v: E.tensor_sub(
                    self.tmp2[:, tt * 512:(tt + 1) * 512], rv[:, tt * 512:(tt + 1) * 512], mu_v), inc=self.sVE)
                P.wait("vector", self.sVE, s1)
                m = P.op("vector", lambda E, kc=kc, tt=tt, rs_v=rs_v: E.tensor_mul(
                    self.arA[:, kc * 1024 + tt * 512: kc * 1024 + (tt + 1) * 512],
                    self.tmp2[:, tt * 512:(tt + 1) * 512], rs_v), inc=self.sVE)
            ve_mark[kc] = m
        self._xhat_done = ve_mark[31]

    def proj(self, w_dram, OUT, sink, cast=True, rhs_buf=None):
        P = self.P
        self.phase_guard()
        rhs = rhs_buf if rhs_buf is not None else self.arA
        nmb = OUT // 128
        wv = [self.wblk[:, 0:4096], self.wblk[:, 4096:8192]]
        wview = w_dram.ap().rearrange("(kc p) m -> p kc m", p=128)
        pe_mark = [0] * nmb
        dma_eng = "gpsimd" if cast else "sync"
        dma_sem = self.sDG if cast else self.sDS
        self._sink_rel = {}
        for mb in range(nmb):
            b = mb % 2
            if mb >= 2:
                P.wait(dma_eng, self.sPE, pe_mark[mb - 2])
            wq_mark = P.dma(dma_eng, wv[b][:, :].rearrange("p (kc m) -> p kc m", kc=32),
                            wview[:, :, mb * 128:(mb + 1) * 128], dma_sem)
            P.wait("tensor", dma_sem, wq_mark)
            pv = self.ps_a[:, b * 1024: b * 1024 + 1024]
            if mb >= 2:
                sem, v = self._sink_rel[mb - 2]
                P.wait("tensor", sem, v)
            m = 0
            for kc in range(32):
                st = kc == 0
                sp = kc == 31
                P.op("tensor", lambda E, b=b, kc=kc, st=st, sp=sp, pv=pv, rhs=rhs: E.matmul(
                    pv[:, 0:512], wv[b][:, kc * 128:(kc + 1) * 128],
                    rhs[:, kc * 1024:kc * 1024 + 512], start=st, stop=sp))
                m = P.op("tensor", lambda E, b=b, kc=kc, st=st, sp=sp, pv=pv, rhs=rhs: E.matmul(
                    pv[:, 512:1024], wv[b][:, kc * 128:(kc + 1) * 128],
                    rhs[:, kc * 1024 + 512:kc * 1024 + 1024], start=st, stop=sp),
                    inc=self.sPE if sp else None)
            pe_mark[mb] = m
            sink(mb, pv, m)

    def sink_rope_store(self, dst_dram, col_off=0, ncols=TPC):
        P = self.P
        st_mark = {}

        def sink(mb, pv, pm):
            b = mb % 2
            P.wait("vector", self.sPE, pm)
            if mb >= 2:
                P.wait("vector", self.sDS, st_mark[mb - 2])
            t1 = self.tmp2[:, 0:1024]
            t2 = self.tmp2[:, 1024:2048]
            P.op("vector", lambda E, pv=pv: E.tensor_mul(t1[:, :], pv[:, :], self.ropeC[:, :]))
            P.op("vector", lambda E, pv=pv: E.tensor_mul(t2[0:64, :], pv[64:128, :], self.ropeS[0:64, :]))
            rel = P.op("vector", lambda E, pv=pv: E.tensor_mul(t2[64:128, :], pv[0:64, :], self.ropeS[64:128, :]), inc=self.sVE)
            P.wait("vector", self.sVE, rel)
            vm = P.op("vector", lambda E, b=b: E.tensor_add(
                self.stg[:, b * 1024:(b + 1) * 1024], t1[:, :], t2[:, :]), inc=self.sVE)
            self._sink_rel[mb] = (self.sVE, rel)
            P.wait("sync", self.sVE, vm)
            st_mark[mb] = P.dma("sync", dst_dram[mb * 128:(mb + 1) * 128, col_off:col_off + ncols],
                                self.stg[:, b * 1024:(b + 1) * 1024], self.sDS)
        return sink

    def sink_store(self, dst_dram):
        P = self.P
        st_mark = {}

        def sink(mb, pv, pm):
            b = mb % 2
            P.wait("vector", self.sPE, pm)
            if mb >= 2:
                P.wait("vector", self.sDS, st_mark[mb - 2])
            vm = P.op("vector", lambda E, pv=pv, b=b: E.tensor_copy(
                self.stg[:, b * 1024:(b + 1) * 1024], pv[:, :]), inc=self.sVE)
            self._sink_rel[mb] = (self.sVE, vm)
            P.wait("sync", self.sVE, vm)
            st_mark[mb] = P.dma("sync", dst_dram[mb * 128:(mb + 1) * 128, :],
                                self.stg[:, b * 1024:(b + 1) * 1024], self.sDS)
        return sink

    def sink_residual(self, rsrc, rdst, stats, final_out=None):
        P = self.P
        st_mark = {}
        po_mark = {}
        pe_stats = {}
        xt = [self.xt2[:, 0:1024], self.xt2[:, 1024:2048]]
        xq = [self.xt2[:, 2048:3072], self.xt2[:, 3072:4096]]

        def sink(mb, pv, pm):
            b = mb % 2
            rold = self.rch[:, b * 1024:(b + 1) * 1024]
            rnew = self.rch[:, 2048 + b * 1024:2048 + (b + 1) * 1024]
            if mb >= 2:
                P.wait("sync", self.sVE, self._sink_rel[mb - 2][1])
            ld = P.dma("sync", rold[:, :], rsrc[mb * 128:(mb + 1) * 128, :], self.sDS)
            P.wait("vector", self.sPE, pm)
            P.wait("vector", self.sDS, ld)
            if mb >= 2:
                P.wait("vector", self.sDS, st_mark[mb - 2])
                if stats:
                    P.wait("vector", self.sPO, po_mark[mb - 2])
            vm = P.op("vector", lambda E, pv=pv, rold=rold, rnew=rnew: E.tensor_add(
                rnew[:, :], pv[:, :], rold[:, :]), inc=self.sVE)
            self._sink_rel[mb] = (self.sVE, vm)
            P.wait("sync", self.sVE, vm)
            dst = final_out if final_out is not None else rdst
            st_mark[mb] = P.dma("sync", dst[mb * 128:(mb + 1) * 128, :], rnew[:, :], self.sDS)
            if stats:
                P.wait("gpsimd", self.sVE, vm)
                if mb >= 2:
                    P.wait("gpsimd", self.sPE, pe_stats[mb - 2])
                P.op("gpsimd", lambda E, rnew=rnew, b=b: E.tensor_copy(xt[b][:, :], rnew[:, :]))
                po_mark[mb] = P.op("gpsimd", lambda E, rnew=rnew, b=b: E.tensor_mul(
                    xq[b][:, :], rnew[:, :], rnew[:, :]), inc=self.sPO)
                P.wait("tensor", self.sPO, po_mark[mb])
                st = mb == 0
                sp = mb == 31
                P.op("tensor", lambda E, b=b, st=st, sp=sp: E.matmul(
                    self.ps_b[0:1, 0:512], self.ones_bf[:, :], xt[b][:, 0:512], start=st, stop=sp))
                P.op("tensor", lambda E, b=b, st=st, sp=sp: E.matmul(
                    self.ps_b[0:1, 512:1024], self.ones_bf[:, :], xt[b][:, 512:1024], start=st, stop=sp))
                P.op("tensor", lambda E, b=b, st=st, sp=sp: E.matmul(
                    self.ps_b[0:1, 1024:1536], self.ones_bf[:, :], xq[b][:, 0:512], start=st, stop=sp))
                pe_stats[mb] = P.op("tensor", lambda E, b=b, st=st, sp=sp: E.matmul(
                    self.ps_b[0:1, 1536:2048], self.ones_bf[:, :], xq[b][:, 512:1024], start=st, stop=sp),
                    inc=self.sPE)
                if mb == 31:
                    self._ln_stats_done = pe_stats[31]
        return sink

    def vproj(self, w_dram, OUT, dst_dram, row_off=0):
        P = self.P
        self.phase_guard()
        nob = OUT // 256
        vr = [self.arB[:, 0:8192], self.arB[:, 8192:16384]]
        wview = w_dram.ap().rearrange("(kc p) m -> p kc m", p=128)
        pe_done = {}
        ve_done = {}
        st_done = {}
        for ob in range(nob):
            b = ob % 2
            if ob >= 2:
                P.wait("gpsimd", self.sPE, pe_done[(ob - 2, 7)])
            wq_mark = P.dma("gpsimd", vr[b][:, :].rearrange("p (kc m) -> p kc m", kc=32),
                            wview[:, :, ob * 256:(ob + 1) * 256], self.sDG)
            P.wait("tensor", self.sDG, wq_mark)
            for tb in range(8):
                sl = tb % 4
                pv = self.ps_a[:, sl * 512: sl * 512 + 256]
                idx = ob * 8 + tb
                if idx >= 4:
                    pi = idx - 4
                    P.wait("tensor", self.sVE, ve_done[(pi // 8, pi % 8)])
                m = 0
                for kc in range(32):
                    st = kc == 0
                    sp = kc == 31
                    m = P.op("tensor", lambda E, b=b, kc=kc, tb=tb, st=st, sp=sp, pv=pv: E.matmul(
                        pv[:, :], self.arA[:, kc * 1024 + tb * 128: kc * 1024 + (tb + 1) * 128],
                        vr[b][:, kc * 256:(kc + 1) * 256], start=st, stop=sp),
                        inc=self.sPE if sp else None)
                pe_done[(ob, tb)] = m
                P.wait("vector", self.sPE, m)
                ss = self.stg[:, sl * 256:(sl + 1) * 256]
                if idx >= 4:
                    pi = idx - 4
                    P.wait("vector", self.sDS, st_done[(pi // 8, pi % 8)])
                vm = P.op("vector", lambda E, pv=pv, ss=ss: E.tensor_copy(ss[:, :], pv[:, :]), inc=self.sVE)
                ve_done[(ob, tb)] = vm
                P.wait("sync", self.sVE, vm)
                st_done[(ob, tb)] = P.dma(
                    "sync", dst_dram[row_off + tb * 128:row_off + (tb + 1) * 128, ob * 256:(ob + 1) * 256],
                    ss[:, :], self.sDS)

    def fire_ag(self, pairs, sem):
        P = self.P
        P.wait("gpsimd", self.sDS, P.val(self.sDS))
        for (i, o) in pairs:
            P.op("gpsimd", lambda E, i=i, o=o: E.collective_compute(
                "AllGather", ALU.bypass, replica_groups=[list(range(NCORES))],
                ins=[i.ap().opt()], outs=[o.ap().opt()]), inc=sem, n=1)

    def _knorms(self, kbuf, nk, kn2_idx):
        P = self.P
        P.wait("tensor", self.sVE, getattr(self, "_bank7_dve", 0))
        nch = nk // 512
        red = self.sc[0:1, 32:32 + nch]
        vm = None
        for i in range(nch):
            pm0 = P.op("gpsimd", lambda E, i=i: E.tensor_mul(
                self.ksq[:, :], kbuf[:, i * 512:(i + 1) * 512], kbuf[:, i * 512:(i + 1) * 512]),
                inc=self.sPO)
            P.wait("tensor", self.sPO, pm0)
            if vm is not None:
                P.wait("tensor", self.sVE, vm)
            pm = P.op("tensor", lambda E: E.matmul(self.ps_b[0:1, 1536:2048], self.ones_bf[:, :],
                                                   self.ksq[:, :], start=True, stop=True), inc=self.sPE)
            P.wait("vector", self.sPE, pm)
            vm = P.op("vector", lambda E, i=i: E.reduce_max(red[:, i:i + 1], self.ps_b[0:1, 1536:2048], axis=AXX), inc=self.sVE)
            self._bank7_dve = vm
            P.wait("gpsimd", self.sPE, pm)
        out = self.sc[0:1, kn2_idx:kn2_idx + 1]
        P.wait("vector", self.sVE, vm)
        P.op("vector", lambda E, nch=nch: E.reduce_max(out[:, :], red[:, 0:nch], axis=AXX), inc=self.sVE)
        return None

    def _qnorms(self, qv, n):
        P = self.P
        if getattr(self, "_qsq_last_mm", 0):
            P.wait("gpsimd", self.sPE, self._qsq_last_mm)
        P.wait("tensor", self.sVE, getattr(self, "_bank7_dve", 0))
        pm0 = P.op("gpsimd", lambda E, qv=qv, n=n: E.tensor_mul(self.qsq[:, 0:n], qv, qv), inc=self.sPO)
        P.wait("tensor", self.sPO, pm0)
        last = None
        for i in range((n + 511) // 512):
            w = min(512, n - i * 512)
            pm = P.op("tensor", lambda E, i=i, w=w: E.matmul(
                self.ps_b[0:1, 1536:1536 + w], self.ones_bf[:, :], self.qsq[:, i * 512:i * 512 + w],
                start=True, stop=True), inc=self.sPE)
            self._qsq_last_mm = pm
            P.wait("vector", self.sPE, pm)
            last = P.op("vector", lambda E, i=i, w=w: E.tensor_copy(
                self.tmpv[0:1, i * 512:i * 512 + w], self.ps_b[0:1, 1536:1536 + w]), inc=self.sVE)
            self._bank7_dve = last
            P.wait("tensor", self.sVE, last)
        return last

    def _mhat(self, qn2_view, kn2_idx, slot, bias_guard):
        P = self.P
        s0 = self.sc[0:1, 8 + slot * 4: 9 + slot * 4]
        P.wait("vector", self.sVE, P.val(self.sVE))
        r1 = P.op("vector", lambda E, qn2_view=qn2_view: E.reduce_max(s0[:, :], qn2_view, axis=AXX), inc=self.sVE)
        P.wait("vector", self.sVE, r1)
        vm = P.op("vector", lambda E: E.tensor_mul(s0[:, :], s0[:, :], self.sc[0:1, kn2_idx:kn2_idx + 1]), inc=self.sVE)
        P.wait("scalar", self.sVE, vm)
        a1 = P.op("scalar", lambda E: E.activation(s0[:, :], s0[:, :], AF.Ln), inc=self.sAE)
        P.wait("scalar", self.sAE, a1)
        am = P.op("scalar", lambda E: E.activation(s0[:, :], s0[:, :], AF.Exp, scale=0.5), inc=self.sAE)
        P.wait("vector", self.sAE, am)
        vm = P.op("vector", lambda E: E.tensor_single_scalar(s0[:, :], s0[:, :], -SM, ALU.mult), inc=self.sVE)
        P.wait("tensor", self.sVE, vm)
        pm = P.op("tensor", lambda E: E.matmul(self.ps_b[:, 1024:1025], self.ones_f[:, :], s0[:, :],
                                               start=True, stop=True), inc=self.sPE)
        P.wait("vector", self.sPE, pm)
        if bias_guard:
            P.wait("vector", self.sAE, bias_guard)
        bm = P.op("vector", lambda E, slot=slot: E.tensor_copy(self.biasb[:, slot:slot + 1], self.ps_b[:, 1024:1025]), inc=self.sVE)
        return bm

    def attn_a(self):
        P = self.P
        self.phase_guard()
        Ka = [self.arA[:, 0:2048], self.arA[:, 4096:6144]]
        Va = [self.arA[:, 2048:4096], self.arA[:, 6144:8192]]
        vview = self.vabuf.ap().rearrange("(tt p) (kv d) -> kv p tt d", tt=16, p=128, kv=8)
        comp_done = {}
        ostg_marks = []
        qslot_last = {}
        bias_last_exp = {}
        o_evac_last = 0
        last_exp_all = 0
        last_pv_all = 0
        for kvh in range(8):
            b = kvh % 2
            if kvh >= 2:
                P.wait("sync", self.sPE, comp_done[kvh - 2])
            P.dma("sync", Ka[b][:, :], self.kabuf[kvh * 128:(kvh + 1) * 128, :], self.sDS)
            kv_ld = P.dma("sync", Va[b][:, :].rearrange("p (tt d) -> p tt d", tt=16),
                          vview[kvh], self.sDS)
            P.wait("gpsimd", self.sDS, kv_ld)
            if not _CONST_BIAS[0]:
                self._knorms(Ka[b], 2048, b)
            elif kvh == 0:
                self._cb_mark = self.P.op("vector", lambda E: E.memset(self.biasb[:, :], -12.0), inc=self.sVE)
            for qh4 in range(4):
                qh = kvh * 4 + qh4
                qs = qh4 % 2
                qv = self.qh[:, qs * 1024:(qs + 1) * 1024]
                if qs in qslot_last:
                    P.wait("sync", self.sPE, qslot_last[qs])
                qld = P.dma("sync", qv, self.qbuf[qh * 128:(qh + 1) * 128, :], self.sDS)
                P.wait("gpsimd", self.sDS, qld)
                if not _CONST_BIAS[0]:
                    self._qnorms(qv, 1024)
                for qt in range(2):
                    kts = [qt * 4 + i for i in range(NKT_A)]
                    masked = MASKED_A[qt]
                    if not _CONST_BIAS[0]:
                        bmark = self._mhat(self.tmpv[0:1, qt * 512:(qt + 1) * 512], b, qt,
                                           bias_last_exp.get(qt))
                        P.wait("scalar", self.sVE, bmark)
                    else:
                        P.wait("scalar", self.sVE, self._cb_mark)
                    P.wait("tensor", self.sPO, P.val(self.sPO) * 0)
                    if last_pv_all:
                        P.wait("scalar", self.sPE, last_pv_all)
                    if last_exp_all:
                        P.wait("tensor", self.sAE, last_exp_all)
                    if o_evac_last:
                        P.wait("tensor", self.sVE, o_evac_last)
                    qvt = qv[:, qt * 512:(qt + 1) * 512]
                    exp_marks = {}
                    pv_marks = {}
                    msk_gate = {}
                    o_first = True
                    ngrp = NKT_A // 2
                    for gi in range(ngrp):
                        gp = gi % 2
                        sv = self.ps_a[:, gp * 1024: gp * 1024 + 1024]
                        if gi >= 2:
                            P.wait("tensor", self.sAE, exp_marks[gi - 2])
                        m = 0
                        for j in range(2):
                            ktloc = kts[gi * 2 + j]
                            m = P.op("tensor", lambda E, b=b, ktloc=ktloc, j=j, gp=gp, qvt=qvt: E.matmul(
                                self.ps_a[:, gp * 1024 + j * 512: gp * 1024 + (j + 1) * 512],
                                Ka[b][:, ktloc * 128:(ktloc + 1) * 128], qvt, start=True, stop=True),
                                inc=self.sPE if j == 1 else None)
                        P.wait("scalar", self.sPE, m)
                        if gi >= 2:
                            P.wait("scalar", self.sPE, pv_marks[gi - 2])
                        em = P.op("scalar", lambda E, gp=gp, sv=sv, qt=qt: E.activation(
                            self.ptile[:, gp * 1024: gp * 1024 + 1024], sv, AF.Exp,
                            bias=self.biasb[:, qt:qt + 1], scale=SM), inc=self.sAE)
                        exp_marks[gi] = em
                        bias_last_exp[qt] = em
                        mgrp = [j for j in range(2) if gi * 2 + j in masked]
                        if mgrp:
                            P.wait("vector", self.sAE, em)
                            mm_ = 0
                            for j in mgrp:
                                mi = MASK_OFF[qt] + masked.index(gi * 2 + j)
                                moff = mi * 512
                                mm_ = P.op("vector", lambda E, gp=gp, j=j, moff=moff: E.tensor_mul(
                                    self.ptile[:, gp * 1024 + j * 512: gp * 1024 + (j + 1) * 512],
                                    self.ptile[:, gp * 1024 + j * 512: gp * 1024 + (j + 1) * 512],
                                    self.maskT[:, moff:moff + 512]), inc=self.sVE)
                            P.wait("tensor", self.sVE, mm_)
                            msk_gate[gi] = mm_
                        else:
                            P.wait("tensor", self.sAE, em)
                        m = 0
                        for j in range(2):
                            ktloc = kts[gi * 2 + j]
                            last = (gi == ngrp - 1) and (j == 1)
                            P.op("tensor", lambda E, b=b, ktloc=ktloc, j=j, gp=gp, of=o_first, sp=last: E.matmul(
                                self.ps_b[:, 0:512], Va[b][:, ktloc * 128:(ktloc + 1) * 128],
                                self.ptile[:, gp * 1024 + j * 512: gp * 1024 + (j + 1) * 512],
                                start=of, stop=sp))
                            m = P.op("tensor", lambda E, j=j, gp=gp, of=o_first, sp=last: E.matmul(
                                self.ps_b[0:1, 512:1024], self.ones_bf[:, :],
                                self.ptile[:, gp * 1024 + j * 512: gp * 1024 + (j + 1) * 512],
                                start=of, stop=sp), inc=self.sPE)
                            o_first = False
                        pv_marks[gi] = m
                    last_pv_all = pv_marks[ngrp - 1]
                    last_exp_all = exp_marks[ngrp - 1]
                    lm = pv_marks[ngrp - 1]
                    P.wait("vector", self.sPE, lm)
                    vm = P.op("vector", lambda E: E.reciprocal(self.musb[0:1, 0:512],
                                                               self.ps_b[0:1, 512:1024]), inc=self.sVE)
                    P.wait("tensor", self.sVE, vm)
                    pm = P.op("tensor", lambda E: E.matmul(self.ps_b[:, 1024:1536], self.ones_f[:, :],
                                                           self.musb[0:1, 0:512], start=True, stop=True),
                              inc=self.sPE)
                    P.wait("vector", self.sPE, pm)
                    cl = P.op("vector", lambda E: E.tensor_copy(self.linvb[:, :], self.ps_b[:, 1024:1536]), inc=self.sVE)
                    nst = len(ostg_marks)
                    sslot = nst % 2
                    ss = self.stg[:, sslot * 1024: sslot * 1024 + 512]
                    if nst >= 2:
                        P.wait("vector", self.sDS, ostg_marks[nst - 2])
                    P.wait("vector", self.sVE, cl)
                    o_evac_last = P.op("vector", lambda E, ss=ss: E.tensor_mul(
                        ss[:, :], self.ps_b[:, 0:512], self.linvb[:, :]), inc=self.sVE)
                    P.wait("sync", self.sVE, o_evac_last)
                    ostg_marks.append(P.dma(
                        "sync", self.obuf[qh * 128:(qh + 1) * 128, qt * 512:(qt + 1) * 512], ss, self.sDS))
                qslot_last[qs] = P.val(self.sPE)
            comp_done[kvh] = P.val(self.sPE)

    def attn_b(self):
        P = self.P
        self.phase_guard()
        Kb = [self.arA[:, 0:8192], self.arA[:, 16384:24576]]
        Vb = [self.arA[:, 8192:16384], self.arA[:, 24576:32768]]
        kview = self.agKb_o.ap().rearrange("(r hh p) t -> hh p r t", r=8, hh=32, p=128)
        vview = self.agVb_o.ap().rearrange("(r tt p) (hh d) -> hh r p tt d", r=8, tt=8, p=128, hh=32)
        comp_done = {}
        ostg_marks = []
        bias_last_exp = {}
        o_evac_last = 0
        last_exp_all = 0
        last_pv_all = 0
        for h in range(32):
            b = h % 2
            if h >= 2:
                P.wait("sync", self.sPE, comp_done[h - 2])
            P.dma("sync", Kb[b][:, :].rearrange("p (r t) -> p r t", r=8), kview[h], self.sDS)
            for r in range(8):
                P.dma("sync", Vb[b][:, r * 1024:(r + 1) * 1024].rearrange("p (tt d) -> p tt d", tt=8),
                      vview[h][r], self.sDS)
            qv = self.qh[:, b * 1024:(b + 1) * 1024]
            kv_ld = P.dma("sync", qv, self.qbuf[h * 128:(h + 1) * 128, :], self.sDS)
            P.wait("gpsimd", self.sDS, kv_ld)
            self._knorms(Kb[b], 8192, b)
            self._qnorms(qv, 1024)
            for qt in range(2):
                bmark = self._mhat(self.tmpv[0:1, qt * 512:(qt + 1) * 512], b, qt,
                                   bias_last_exp.get(qt))
                P.wait("scalar", self.sVE, bmark)
                if last_pv_all:
                    P.wait("scalar", self.sPE, last_pv_all)
                if last_exp_all:
                    P.wait("tensor", self.sAE, last_exp_all)
                if o_evac_last:
                    P.wait("tensor", self.sVE, o_evac_last)
                qvt = qv[:, qt * 512:(qt + 1) * 512]
                exp_marks = {}
                pv_marks = {}
                o_first = True
                for gi in range(32):
                    gp = gi % 2
                    sv = self.ps_a[:, gp * 1024: gp * 1024 + 1024]
                    if gi >= 2:
                        P.wait("tensor", self.sAE, exp_marks[gi - 2])
                    m = 0
                    for j in range(2):
                        kt = gi * 2 + j
                        m = P.op("tensor", lambda E, b=b, kt=kt, j=j, gp=gp, qvt=qvt: E.matmul(
                            self.ps_a[:, gp * 1024 + j * 512: gp * 1024 + (j + 1) * 512],
                            Kb[b][:, kt * 128:(kt + 1) * 128], qvt, start=True, stop=True),
                            inc=self.sPE if j == 1 else None)
                    P.wait("scalar", self.sPE, m)
                    if gi >= 2:
                        P.wait("scalar", self.sPE, pv_marks[gi - 2])
                    em = P.op("scalar", lambda E, gp=gp, sv=sv, qt=qt: E.activation(
                        self.ptile[:, gp * 1024: gp * 1024 + 1024], sv, AF.Exp,
                        bias=self.biasb[:, qt:qt + 1], scale=SM), inc=self.sAE)
                    exp_marks[gi] = em
                    bias_last_exp[qt] = em
                    P.wait("tensor", self.sAE, em)
                    m = 0
                    for j in range(2):
                        kt = gi * 2 + j
                        last = (gi == 31) and (j == 1)
                        P.op("tensor", lambda E, b=b, kt=kt, j=j, gp=gp, of=o_first, sp=last: E.matmul(
                            self.ps_b[:, 0:512], Vb[b][:, kt * 128:(kt + 1) * 128],
                            self.ptile[:, gp * 1024 + j * 512: gp * 1024 + (j + 1) * 512],
                            start=of, stop=sp))
                        m = P.op("tensor", lambda E, j=j, gp=gp, of=o_first, sp=last: E.matmul(
                            self.ps_b[0:1, 512:1024], self.ones_bf[:, :],
                            self.ptile[:, gp * 1024 + j * 512: gp * 1024 + (j + 1) * 512],
                            start=of, stop=sp), inc=self.sPE)
                        o_first = False
                    pv_marks[gi] = m
                last_pv_all = pv_marks[31]
                last_exp_all = exp_marks[31]
                lm = pv_marks[31]
                P.wait("vector", self.sPE, lm)
                vm = P.op("vector", lambda E: E.reciprocal(self.musb[0:1, 0:512],
                                                           self.ps_b[0:1, 512:1024]), inc=self.sVE)
                P.wait("tensor", self.sVE, vm)
                pm = P.op("tensor", lambda E: E.matmul(self.ps_b[:, 1024:1536], self.ones_f[:, :],
                                                       self.musb[0:1, 0:512], start=True, stop=True),
                          inc=self.sPE)
                P.wait("vector", self.sPE, pm)
                cl = P.op("vector", lambda E: E.tensor_copy(self.linvb[:, :], self.ps_b[:, 1024:1536]), inc=self.sVE)
                nst = len(ostg_marks)
                sslot = nst % 2
                ss = self.stg[:, sslot * 1024: sslot * 1024 + 512]
                if nst >= 2:
                    P.wait("vector", self.sDS, ostg_marks[nst - 2])
                P.wait("vector", self.sVE, cl)
                o_evac_last = P.op("vector", lambda E, ss=ss: E.tensor_mul(
                    ss[:, :], self.ps_b[:, 0:512], self.linvb[:, :]), inc=self.sVE)
                P.wait("sync", self.sVE, o_evac_last)
                ostg_marks.append(P.dma(
                    "sync", self.obuf[h * 128:(h + 1) * 128, qt * 512:(qt + 1) * 512], ss, self.sDS))
            comp_done[h] = P.val(self.sPE)

    def load_obuf(self):
        P = self.P
        m = 0
        for i in range(4):
            m = P.dma("sync", self.arA[:, i * 8192:(i + 1) * 8192].rearrange("p (kc t) -> p kc t", kc=8),
                      self.obuf.ap().rearrange("(kc p) t -> p kc t", p=128)[:, i * 8:(i + 1) * 8, :], self.sDS)
        P.wait("tensor", self.sDS, m)

    def load_rope_own(self):
        P = self.P
        P.dma("sync", self.ropeC[:, :], self.ropeC_t[:, :], self.sDS)
        P.dma("sync", self.ropeS[:, :], self.ropeS_t[:, :], self.sDS)

    def emit_copy_out(self, src):
        self.P.dma("sync", self.y_t[:, :], src[:, :], self.sDS)

    def emit_dump_bf16(self, src_ap, nelem):
        rows = nelem // TPC
        self.P.dma("gpsimd", self.y_t[0:rows, :], src_ap, self.sDG)

    def finalize(self):
        P = self.P
        P.wait("sync", self.sDS, P.val(self.sDS))
        P.wait("sync", self.sDG, P.val(self.sDG))

    def build(self):
        P = self.P
        ds = self.debug_stop
        self.init_consts()
        if ds == 19:
            self.emit_copy_out(self.x_t)
            self.finalize()
            return
        self.dequant()
        if ds == 20:
            self.fire_agw()
            P.wait("sync", self.sCW, 1)
            self.emit_copy_out(self.x_t)
            self.finalize()
            return
        if ds == 27:
            self.emit_copy_out(self.x_t)
            self.finalize()
            return
        self.ln_stats_standalone(self.xp_t, guard_deq=True)
        self.barrier()
        self.ln_finalize_apply(self.xp_t)
        self.barrier()
        if ds == 21:
            self.emit_copy_out(self.x_t)
            self.finalize()
            return
        if ds == 30:
            for kc in range(32):
                P.dma("gpsimd", self.y_t[kc * 128:(kc + 1) * 128, :],
                      self.arA[:, kc * 1024:(kc + 1) * 1024], self.sDG)
            self.finalize()
            return
        self.proj(self.wk_t, 1024, self.sink_rope_store(self.kabuf, col_off=0))
        self.barrier()
        if ds == 22:
            self.emit_copy_out(self.x_t)
            self.finalize()
            return
        if ds == 31:
            self.emit_dump_bf16(self.kabuf[:, :], 1024 * 2048)
            self.finalize()
            return
        self.vproj(self.wv_t, 1024, self.vabuf, row_off=0)
        self.barrier()
        if ds == 23:
            self.emit_copy_out(self.x_t)
            self.finalize()
            return
        if ds == 32:
            self.emit_dump_bf16(self.vabuf[:, :], 2048 * 1024)
            self.finalize()
            return
        self.ln_stats_standalone(self.x_t)
        self.barrier()
        self.ln_finalize_apply(self.x_t)
        self.load_rope_own()
        self.barrier()
        self.proj(self.wk_t, 1024, self.sink_rope_store(self.kabuf, col_off=1024))
        self.barrier()
        self.vproj(self.wv_t, 1024, self.vabuf, row_off=1024)
        self.barrier()
        self.proj(self.wq_t, 4096, self.sink_rope_store(self.qbuf))
        if ds != 47:
            self.fire_agw()
        self.barrier()
        if ds == 44:
            self.proj(self.wo_t, 4096, self.sink_store(self.obuf))
            self.barrier()
            self.emit_dump_bf16(self.obuf[:, :], 4096 * TPC)
            self.finalize()
            return
        if ds == 24:
            self.emit_copy_out(self.x_t)
            self.finalize()
            return
        if ds == 33:
            self.emit_dump_bf16(self.qbuf[:, :], 4096 * TPC)
            self.finalize()
            return
        if ds == 36:
            self.emit_dump_bf16(self.kabuf[:, :], 1024 * 2048)
            self.finalize()
            return
        if ds == 37:
            self.emit_dump_bf16(self.vabuf[:, :], 2048 * 1024)
            self.finalize()
            return
        self.attn_a()
        self.barrier()
        if ds == 25:
            self.emit_copy_out(self.x_t)
            self.finalize()
            return
        if ds == 34:
            self.emit_dump_bf16(self.obuf[:, :], 4096 * TPC)
            self.finalize()
            return
        self.load_obuf()
        self.barrier()
        if ds == 39:
            for kc in range(32):
                P.dma("gpsimd", self.y_t[kc * 128:(kc + 1) * 128, :],
                      self.arA[:, kc * 1024:(kc + 1) * 1024], self.sDG)
            self.finalize()
            return
        if ds in (46, 47):
            wview = self.wo_t.ap().rearrange("(kc p) m -> p kc m", p=128)
            m = P.dma("gpsimd", self.wblk[:, 0:4096].rearrange("p (kc m) -> p kc m", kc=32),
                      wview[:, :, 0:128], self.sDG)
            P.wait("tensor", self.sDG, m)
            pm = 0
            for kc in range(32):
                pm = P.op("tensor", lambda E, kc=kc, st=(kc == 0), sp=(kc == 31): E.matmul(
                    self.ps_a[:, 0:512], self.wblk[:, kc * 128:(kc + 1) * 128],
                    self.arA[:, kc * 1024:kc * 1024 + 512], start=st, stop=sp),
                    inc=self.sPE if kc == 31 else None)
            P.wait("vector", self.sPE, pm)
            vm = P.op("vector", lambda E: E.tensor_copy(self.tmp2[:, 0:512], self.ps_a[:, 0:512]), inc=self.sVE)
            P.wait("sync", self.sVE, vm)
            P.dma("sync", self.y_t[0:128, 0:512], self.tmp2[:, 0:512], self.sDS)
            P.dma("gpsimd", self.y_t[128:256, :], self.arA[:, 0:1024], self.sDG)
            P.dma("gpsimd", self.y_t[256:384, :], self.wblk[:, 0:1024], self.sDG)
            P.dma("gpsimd", self.y_t[384:512, :], self.arA[:, 16384:17408], self.sDG)
            P.dma("gpsimd", self.y_t[512:640, :], self.obuf[0:128, :], self.sDG)
            P.dma("gpsimd", self.y_t[640:768, :], self.obuf[2048:2176, :], self.sDG)
            self.finalize()
            return
        if ds == 45:
            wview = self.wo_t.ap().rearrange("(kc p) m -> p kc m", p=128)
            m = P.dma("gpsimd", self.wblk[:, 0:4096].rearrange("p (kc m) -> p kc m", kc=32),
                      wview[:, :, 0:128], self.sDG)
            P.wait("gpsimd", self.sDG, m)
            P.dma("gpsimd", self.y_t[0:128, :][:, 0:1024], self.wblk[:, 0:1024], self.sDG)
            P.dma("gpsimd", self.y_t[128:256, :][:, 0:1024], self.wblk[:, 1024:2048], self.sDG)
            P.dma("gpsimd", self.y_t[256:384, :][:, 0:1024], self.wblk[:, 2048:3072], self.sDG)
            P.dma("gpsimd", self.y_t[384:512, :][:, 0:1024], self.wblk[:, 3072:4096], self.sDG)
            self.finalize()
            return
        if ds == 43:
            self.proj(self.wo_t, 4096, self.sink_store(self.qbuf), rhs_buf=self.arA)
            self.barrier()
            self.emit_dump_bf16(self.qbuf[:, :], 4096 * TPC)
            self.finalize()
            return
        self.proj(self.wo_t, 4096, self.sink_residual(self.x_t, self.res0, stats=(ds != 1)), rhs_buf=self.arA)
        self.barrier()
        if ds == 1:
            self.emit_copy_out(self.res0)
            self.finalize()
            return
        self.ln_finalize_apply(self.res0)
        self.barrier()
        if ds == 49:
            for kc in range(32):
                P.dma("gpsimd", self.y_t[kc * 128:(kc + 1) * 128, :],
                      self.arA[:, kc * 1024:(kc + 1) * 1024], self.sDG)
            self.finalize()
            return
        if ds == 48:
            P.wait("sync", self.sCW, 1)
            P.dma("gpsimd", self.y_t[:, :], self.agW_o[0:1024, :], self.sDG)
            self.finalize()
            return
        P.wait("sync", self.sCW, 1)
        self.proj(self.agW_o, 4096, self.sink_residual(self.res0, self.res1, stats=(ds != 2)), cast=False)
        self.barrier()
        if ds == 2:
            self.emit_copy_out(self.res1)
            self.finalize()
            return
        self.ln_finalize_apply(self.res1)
        self.barrier()
        self.proj(self.bwk_t, 4096, self.sink_store(self.agKb_i))
        self.barrier()
        self.vproj(self.bwv_t, 4096, self.agVb_i)
        self.fire_ag([(self.agKb_i, self.agKb_o), (self.agVb_i, self.agVb_o)], self.sCB)
        self.barrier()
        self.proj(self.bwq_t, 4096, self.sink_store(self.qbuf))
        self.barrier()
        P.wait("sync", self.sCB, 2)
        self.attn_b()
        self.barrier()
        self.load_obuf()
        self.barrier()
        self.proj(self.bwo_t, 4096, self.sink_residual(self.res1, self.res0, stats=(ds != 3)), rhs_buf=self.arA)
        self.barrier()
        if ds == 3:
            self.emit_copy_out(self.res0)
            self.finalize()
            return
        self.ln_finalize_apply(self.res0)
        self.barrier()
        self.proj(self.agW_o, 4096, self.sink_residual(self.res0, None, stats=False, final_out=self.y_t),
                  cast=False)
        self.finalize()


def build_nc(debug_stop=0):
    from contextlib import ExitStack
    nc = bass.Bass("TRN2", target_bir_lowering=False, debug=False)
    with ExitStack() as ctx:
        mk = MK(nc, ctx, debug_stop=debug_stop)
        mk.build()
        with nc.Block() as blk:
            blk.sync(lambda E: mk.P.emit("sync", E))
            blk.gpsimd(lambda E: mk.P.emit("gpsimd", E))
            blk.tensor(lambda E: mk.P.emit("tensor", E))
            blk.vector(lambda E: mk.P.emit("vector", E))
            blk.scalar(lambda E: mk.P.emit("scalar", E))
    return nc


def _rope_tables(pos):
    f = np.arange(64, dtype=np.float64)
    inv = 10000.0 ** (-f / 64.0)
    ang = pos[None, :].astype(np.float64) * inv[:, None]
    C = np.cos(ang)
    S = np.sin(ang)
    Cf = np.ascontiguousarray(np.concatenate([C, C], axis=0).astype(np.float32))
    Sf = np.ascontiguousarray(np.concatenate([-S, S], axis=0).astype(np.float32))
    return Cf, Sf


def _masks_for_core(c):
    tiles = []
    q = np.arange(QT_A)
    k = np.arange(128)
    for qt in range(2):
        for ki in MASKED_A[qt]:
            dlt = ki * 128 - 1024
            qg = c * 1024 + qt * QT_A + q
            kg = c * 1024 + qt * QT_A + dlt + k
            diff = qg[None, :] - kg[:, None]
            allowed = (diff >= 0) & (diff < 1024) & (kg[:, None] >= 0)
            tiles.append(allowed.astype(np.float32))
    m = np.concatenate(tiles, axis=1)
    return np.ascontiguousarray(m.astype(ml_dtypes.bfloat16))


def kernel(L0_attn_a_x, L0_attn_a_wq, L0_attn_a_wk, L0_attn_a_wv, L0_attn_a_wo,
           L0_attn_b_Wq, L0_attn_b_Wk, L0_attn_b_Wv, L0_attn_b_Wo,
           L0_ffn_qweight, L0_ffn_scale, L0_ffn_zero):
    x = np.ascontiguousarray(np.asarray(L0_attn_a_x, dtype=np.float32)[0])
    wq = np.ascontiguousarray(np.asarray(L0_attn_a_wq, dtype=np.float32))
    wk = np.ascontiguousarray(np.asarray(L0_attn_a_wk, dtype=np.float32))
    wv = np.ascontiguousarray(np.asarray(L0_attn_a_wv, dtype=np.float32))
    wo = np.ascontiguousarray(np.asarray(L0_attn_a_wo, dtype=np.float32))
    bwq = np.ascontiguousarray(np.asarray(L0_attn_b_Wq, dtype=np.float32).T)
    bwk = np.ascontiguousarray(np.asarray(L0_attn_b_Wk, dtype=np.float32).T)
    bwv = np.ascontiguousarray(np.asarray(L0_attn_b_Wv, dtype=np.float32).T)
    bwo = np.ascontiguousarray(np.asarray(L0_attn_b_Wo, dtype=np.float32).T)
    qw = np.ascontiguousarray(np.asarray(L0_ffn_qweight, dtype=np.int32))
    sc = np.asarray(L0_ffn_scale, dtype=np.float32)
    zr = np.asarray(L0_ffn_zero, dtype=np.float32)

    ident = np.ascontiguousarray(np.eye(128, dtype=ml_dtypes.bfloat16))

    in_maps = []
    for c in range(NCORES):
        xc = np.ascontiguousarray(x[c * 1024:(c + 1) * 1024].T)
        if c == 0:
            xp = np.zeros((D, TPC), dtype=np.float32)
        else:
            xp = np.ascontiguousarray(x[(c - 1) * 1024:c * 1024].T)
        ropeC, ropeS = _rope_tables(np.arange(c * 1024, (c + 1) * 1024))
        ropeCp, ropeSp = _rope_tables(np.arange((c - 1) * 1024, c * 1024))
        qw_c = np.ascontiguousarray(qw[:, 256 * c:256 * (c + 1)])
        zr_c = np.ascontiguousarray(np.repeat(zr[:, 4 * c:4 * (c + 1)], 128, axis=1))
        sr_c = np.ascontiguousarray(np.repeat(sc[:, 4 * c:4 * (c + 1)], 128, axis=1))
        in_maps.append(dict(
            x_t=xc, xp_t=xp, wq_t=wq, wk_t=wk, wv_t=wv, wo_t=wo,
            bwq_t=bwq, bwk_t=bwk, bwv_t=bwv, bwo_t=bwo,
            qw_t=qw_c, zr_t=zr_c, sr_t=sr_c,
            ropeC_t=ropeC, ropeS_t=ropeS, ropeCp_t=ropeCp, ropeSp_t=ropeSp,
            mask_t=_masks_for_core(c), ident_t=ident,
        ))

    nc = build_nc(debug_stop=_DEBUG_STOP[0])
    res = run_bass_kernel_spmd(nc, in_maps, core_ids=list(range(NCORES)))
    out = np.concatenate([np.asarray(r["y_t"]).T for r in res.results], axis=0)
    return out.reshape(1, SEQ, D).astype(np.float32)


_DEBUG_STOP = [0]
